# version 01
# baseline (speedup 1.0000x reference)
import contextlib
import numpy as np
import ml_dtypes
import concourse.bass as bass
import concourse.mybir as mybir
from concourse.bass_utils import run_bass_kernel_spmd

F32 = mybir.dt.float32
BF16 = mybir.dt.bfloat16
ALU = mybir.AluOpType
AF = mybir.ActivationFunctionType
AX = mybir.AxisListType
BF16NP = ml_dtypes.bfloat16

D = 1024
T = 4096
NH = 16
DFF = 2736
NMS = 22
SEG = 256
NSEG = T // SEG
CH = 64
NCH = SEG // CH
NE = 8
DI = 4096


class Toks:
    def __init__(self, sem, step):
        self.sem, self.step, self.n = sem, step, 0

    def inc(self, ins):
        ins.then_inc(self.sem, self.step)
        self.n += self.step
        return self.n


class Waiter:
    def __init__(self):
        self.seen = {}

    def __call__(self, eng, tk, val):
        if val <= 0:
            return
        key = (id(eng), id(tk.sem))
        if self.seen.get(key, 0) >= val:
            return
        self.seen[key] = val
        eng.wait_ge(tk.sem, val)


def host_consts():
    c = {}
    ho = np.zeros((128, 128), np.float32)
    for t in range(8):
        ho[0:64, 16 * t + 2 * t] = 1.0
        ho[64:128, 16 * t + 2 * t + 1] = 1.0
    c["headones"] = ho
    he = np.zeros((16, 8 * 128), np.float32)
    for t in range(8):
        he[2 * t, 128 * t:128 * t + 64] = 1.0
        he[2 * t + 1, 128 * t + 64:128 * t + 128] = 1.0
    c["headexpand"] = he
    c["ones_row"] = np.ones((1, 128), np.float32)
    c["ones_col"] = np.ones((128, 1), np.float32)
    c["ones_col_bf"] = np.ones((128, 1), BF16NP)
    c["headones_bf"] = c["headones"].astype(BF16NP)
    m = np.zeros((128, 1024), np.float32)
    for h in range(NH):
        for j in range(64):
            m[j, 64 * h + j:64 * h + 64] = 1.0
            m[64 + j, 64 * h + j:64 * h + 64] = 1.0
    c["trilmask"] = m.astype(BF16NP)
    c["ident_f32"] = np.eye(128, dtype=np.float32)
    c["biasvals"] = np.array([[1e-6, 1e-5, 0.0, 0.0]], np.float32)
    return c


CONST_SPECS = {"headones": ([128, 128], F32), "headexpand": ([16, 1024], F32),
               "ones_row": ([1, 128], F32), "ones_col": ([128, 1], F32),
               "ones_col_bf": ([128, 1], BF16), "headones_bf": ([128, 128], BF16),
               "trilmask": ([128, 1024], BF16), "ident_f32": ([128, 128], F32),
               "biasvals": ([1, 4], F32)}

A_AR = 86016
O_WQ, O_WK, O_WV, O_WG, O_WO = 0, 8192, 16384, 24576, 32768
O_QTS, O_KTS, O_VNS, O_KNS = 40960, 45056, 49152, 53248
O_OTS, O_X0S, O_H1S, O_SQB = 57344, 61440, 65536, 67584
O_BP, O_BM, O_KH, O_PSB, O_SST, O_STL = 69632, 71680, 73728, 73984, 75008, 75520
O_H2T = 0
O_W12 = 32768
O_X1B = 36864
O_HFR = 40960
O_XMR = 52224
O_XPR = 56320
O_HFW = 60416
O_ASR = 62464
O_W3S = 63488
O_LSQ = 4096


def build_attn(races=True, phase=4, a1_stop=8, nseg=NSEG, chunk_ops=4):
    nc = bass.Bass("TRN2", target_bir_lowering=False,
                   detect_race_conditions=races)
    C = {k: nc.dram_tensor("c_" + k, s, d, kind="ExternalInput").ap()
         for k, (s, d) in CONST_SPECS.items()}

    x0T = nc.dram_tensor("x0T", [D, T], F32, kind="ExternalInput").ap()
    Wq = nc.dram_tensor("Wq", [D, D], F32, kind="ExternalInput").ap()
    Wk = nc.dram_tensor("Wk", [D, D], F32, kind="ExternalInput").ap()
    Wv = nc.dram_tensor("Wv", [D, D], F32, kind="ExternalInput").ap()
    Wg = nc.dram_tensor("Wg", [D, D], F32, kind="ExternalInput").ap()
    Wo = nc.dram_tensor("Wo", [D, D], F32, kind="ExternalInput").ap()
    w1 = nc.dram_tensor("w1", [D, DFF], F32, kind="ExternalInput").ap()
    w2 = nc.dram_tensor("w2", [D, DFF], F32, kind="ExternalInput").ap()
    w3 = nc.dram_tensor("w3", [DFF, D], F32, kind="ExternalInput").ap()
    g_attn = nc.dram_tensor("g_attn", [1, D], F32, kind="ExternalInput").ap()
    g_ffn = nc.dram_tensor("g_ffn", [1, D], F32, kind="ExternalInput").ap()
    routerT = nc.dram_tensor("routerT", [D, NE], F32, kind="ExternalInput").ap()

    x1T_out = nc.dram_tensor("x1T", [D, T], BF16, kind="ExternalOutput").ap()
    sel_out = nc.dram_tensor("sel", [128, 32 * 8], F32,
                             kind="ExternalOutput").ap()
    part_out = nc.dram_tensor("partials", [1, 40], F32,
                              kind="ExternalOutput").ap()

    xmid_dram = nc.dram_tensor("xmidT_scr", [D, T], BF16).ap()
    hff_dram = nc.dram_tensor("hff_scr", [8, NMS, 128, 512], BF16).ap()

    es = contextlib.ExitStack()
    with es:
        AR = es.enter_context(nc.sbuf_tensor("AR", [128, A_AR], BF16))
        alph = es.enter_context(nc.sbuf_tensor("alph", [128, SEG], F32))
        inva = es.enter_context(nc.sbuf_tensor("inva", [128, SEG], F32))
        elcs = es.enter_context(nc.sbuf_tensor("elcs", [128, 2 * 8 * NCH], F32))
        l2s = es.enter_context(nc.sbuf_tensor("l2s", [16, 2 * SEG], F32))
        rowst = es.enter_context(nc.sbuf_tensor("rowst", [1, 2048], F32))
        grow = es.enter_context(nc.sbuf_tensor("grow", [1, D], F32))
        gfrow = es.enter_context(nc.sbuf_tensor("gfrow", [1, D], F32))
        honesS = es.enter_context(nc.sbuf_tensor("honesS", [128, 128], F32))
        hexpS = es.enter_context(nc.sbuf_tensor("hexpS", [16, 1024], F32))
        onesrowS = es.enter_context(nc.sbuf_tensor("onesrowS", [1, 128], F32))
        onescolS = es.enter_context(nc.sbuf_tensor("onescolS", [128, 1], F32))
        onescolB = es.enter_context(nc.sbuf_tensor("onescolB", [128, 1], BF16))
        honesB = es.enter_context(nc.sbuf_tensor("honesB", [128, 128], BF16))
        kbias = es.enter_context(nc.sbuf_tensor("kbias", [1, 4], F32))
        maskS = es.enter_context(nc.sbuf_tensor("maskS", [128, 1024], BF16))
        identS = es.enter_context(nc.sbuf_tensor("identS", [128, 128], F32))
        khatF = es.enter_context(nc.sbuf_tensor("khatF", [128, SEG], F32))
        routs = es.enter_context(nc.sbuf_tensor("routs", [128, 8 * NE], BF16))
        selacc = es.enter_context(nc.sbuf_tensor("selacc", [128, 32 * 8], F32))
        selw = es.enter_context(nc.sbuf_tensor("selw", [128, 16], F32))
        rtm = es.enter_context(nc.sbuf_tensor("rtm", [128, 16], F32))
        rcols = es.enter_context(nc.sbuf_tensor("rcols", [128, 8], F32))
        ptsb = es.enter_context(nc.sbuf_tensor("ptsb", [16, 1], F32))
        pp = es.enter_context(nc.psum_tensor("pp", [128, 512], F32))
        pbc = es.enter_context(nc.psum_tensor("pbc", [128, 512], F32))
        pv = es.enter_context(nc.psum_tensor("pv", [128, 512], F32))
        pu = es.enter_context(nc.psum_tensor("pu", [128, 512], F32))
        pP = es.enter_context(nc.psum_tensor("pP", [128, 1024], F32))
        pOa = es.enter_context(nc.psum_tensor("pOa", [128, 512], F32))
        pOb = es.enter_context(nc.psum_tensor("pOb", [128, 512], F32))
        s_t = es.enter_context(nc.semaphore("s_t"))
        s_v = es.enter_context(nc.semaphore("s_v"))
        s_s = es.enter_context(nc.semaphore("s_s"))

        def mkq(name):
            return Toks(es.enter_context(nc.semaphore(name)), 16)

        stW = mkq("stW")
        stConst = mkq("stConst")
        stX0 = [mkq("stX0a"), mkq("stX0b")]
        stXm = mkq("stXm")
        stW3 = mkq("stW3")
        stW12 = [mkq("stW12a"), mkq("stW12b")]
        stHffW = [mkq("stHffWa"), mkq("stHffWb")]
        stHfr = mkq("stHfr")
        stXmr = mkq("stXmr")
        stX1 = mkq("stX1")
        stOut = mkq("stOut")
        block = es.enter_context(nc.Block())
        tT, tV, tS = Toks(s_t, 1), Toks(s_v, 1), Toks(s_s, 1)
        W = Waiter()
        eT, eV, eS, eG, eY = (block.tensor, block.vector, block.scalar,
                              block.gpsimd, block.sync)

        def ar(off, n):
            return AR[:, off:off + n]

        WQ, WK, WV, WG, WOx = (ar(o, 8192) for o in
                               (O_WQ, O_WK, O_WV, O_WG, O_WO))
        qts, kts = ar(O_QTS, 4096), ar(O_KTS, 4096)
        vns, kns = ar(O_VNS, 4096), ar(O_KNS, 4096)
        ots, x0s = ar(O_OTS, 4096), ar(O_X0S, 4096)
        h1s, sqb = ar(O_H1S, 2048), ar(O_SQB, 2048)
        bplus, bminus = ar(O_BP, 2048), ar(O_BM, 2048)
        khatT, Psb = khatF, ar(O_PSB, 1024)
        Sst, Stl = ar(O_SST, 512), ar(O_STL, 512)

        def _(g):
            for k in range(8):
                for dst, src in ((WQ, Wq), (WK, Wk), (WV, Wv), (WG, Wg),
                                 (WOx, Wo)):
                    stW.inc(g.dma_start(out=dst[:, D * k:D * (k + 1)],
                                        in_=src[128 * k:128 * (k + 1), :]))
            stW.inc(g.dma_start(
                out=routs[:, :].rearrange("p (kt e) -> p kt e", e=NE),
                in_=routerT.rearrange("(kt p) e -> p kt e", p=128)))
        eG(_)
        qkv_tok = stW.n

        def _(y):
            stConst.inc(y.dma_start(out=grow[:, :], in_=g_attn))
            stConst.inc(y.dma_start(out=gfrow[:, :], in_=g_ffn))
            for dst, key in ((honesS, "headones"), (hexpS, "headexpand"),
                             (onesrowS, "ones_row"), (onescolS, "ones_col"),
                             (onescolB, "ones_col_bf"), (honesB, "headones_bf"),
                             (maskS, "trilmask"), (identS, "ident_f32"),
                             (kbias, "biasvals")):
                stConst.inc(y.dma_start(out=dst[:, :], in_=C[key]))
        eY(_)
        const_tok = stConst.n

        def _(v):
            tV.inc(v.memset(Sst, 0.0))
        eV(_)

        g_ = {
            "pp_v": [0, 0], "pp_s": [0, 0],
            "pbc_v": [0, 0],
            "set_t": [0, 0],
            "ots_t": [0, 0],
            "x0_v": [0, 0],
            "h1_y": 0, "h1_t": 0,
            "sqb_t": 0,
            "pv_v": 0, "pu_v": 0, "pP_v": 0,
            "pOa_v": 0, "pOb_v": 0,
            "psb_t": 0, "stil_t": 0,
            "gate_v": 0,
        }
        prev_O_tok = 0
        xm_last = 0

        for s in range(nseg):
            par = s % 2
            sb = 8 * SEG * par
            sb = 2048 * par
            so = SEG * s

            def _(gq, par=par, so=so):
                W(gq, tV, g_["x0_v"][par])
                stX0[par].inc(gq.dma_start(
                    out=x0s[:, 2048 * par:2048 * par + 2048].rearrange(
                        "p (kt c) -> p kt c", c=SEG),
                    in_=x0T.rearrange("(kt p) t -> p kt t",
                                      p=128)[:, :, so:so + SEG]))
            eG(_)
            x0_tok = stX0[par].n

            def _(v, par=par):
                W(v, stX0[par], x0_tok)
                W(v, tT, g_["sqb_t"])
                for k in range(8):
                    sl = x0s[:, 2048 * par + SEG * k:2048 * par + SEG * (k + 1)]
                    tV.inc(v.tensor_tensor(sqb[:, SEG * k:SEG * (k + 1)],
                                           sl, sl, ALU.mult))
            eV(_)
            sq_tok = tV.n

            def _(t):
                W(t, tV, sq_tok)
                W(t, stConst, const_tok)
                W(t, tS, g_["pp_s"][0])
                W(t, tV, g_["pp_v"][0])
                for k in range(8):
                    ins = t.matmul(pp[0:1, 0:SEG], onescolB[:, :],
                                   sqb[:, SEG * k:SEG * (k + 1)],
                                   start=(k == 0), stop=(k == 7))
                tT.inc(ins)
            eT(_)
            s1_tok = tT.n
            g_["sqb_t"] = s1_tok

            def _(s_):
                W(s_, tT, s1_tok)
                W(s_, stConst, const_tok)
                tS.inc(s_.activation(rowst[0:1, 0:SEG], pp[0:1, 0:SEG],
                                     AF.Ln, bias=kbias[0:1, 0:1], scale=1.0 / D))
                tS.inc(s_.activation(rowst[0:1, SEG:2 * SEG],
                                     rowst[0:1, 0:SEG], AF.Exp, scale=-0.5))
            eS(_)
            rstd_tok = tS.n
            g_["pp_s"][0] = rstd_tok

            h1_tok = 0
            for t8 in range(8):
                pvb = pv if t8 % 2 == 0 else pu
                gk = "pv_v" if t8 % 2 == 0 else "pu_v"

                def _(t, t8=t8, pvb=pvb, gk=gk):
                    W(t, tS, rstd_tok)
                    W(t, stConst, const_tok)
                    W(t, tV, g_[gk])
                    tT.inc(t.matmul(pvb[:, 0:SEG],
                                    grow[0:1, 128 * t8:128 * (t8 + 1)],
                                    rowst[0:1, SEG:2 * SEG],
                                    start=True, stop=True))
                eT(_)
                bc_tok = tT.n

                def _(v, t8=t8, pvb=pvb, gk=gk, par=par):
                    W(v, tT, bc_tok)
                    W(v, tT, g_["h1_t"])
                    W(v, stXm, g_["h1_y"])
                    tV.inc(v.tensor_tensor(
                        h1s[:, SEG * t8:SEG * (t8 + 1)],
                        x0s[:, 2048 * par + SEG * t8:
                            2048 * par + SEG * (t8 + 1)],
                        pvb[:, 0:SEG], ALU.mult))
                eV(_)
                g_[gk] = tV.n
                h1_tok = tV.n

            if a1_stop < 2:
                continue

            def proj_qk(dstbuf, Wsl):
                last = 0
                for mt in range(8):
                    half = mt % 2
                    pj = pp if half == 0 else pbc

                    def _(t, mt=mt, half=half, pj=pj):
                        W(t, tV, h1_tok)
                        W(t, stW, qkv_tok)
                        W(t, tV, g_["pp_v"][half])
                        W(t, tS, g_["pp_s"][half])
                        W(t, tT, g_["set_t"][par])
                        for k in range(8):
                            ins = t.matmul(
                                pj[:, 0:SEG],
                                Wsl[:, D * k + 128 * mt:D * k + 128 * (mt + 1)],
                                h1s[:, SEG * k:SEG * (k + 1)],
                                start=(k == 0), stop=(k == 7))
                        tT.inc(ins)
                    eT(_)
                    mm_tok = tT.n

                    def _(s_, mt=mt, half=half, pj=pj):
                        W(s_, tT, mm_tok)
                        tS.inc(s_.copy(
                            dstbuf[:, sb + SEG * mt:sb + SEG * (mt + 1)],
                            pj[:, 0:SEG]))
                    eS(_)
                    g_["pp_s"][half] = tS.n
                    last = tS.n
                return last

            q_tok = proj_qk(qts, WQ)
            k_tok = proj_qk(kts, WK)

            if a1_stop < 3:
                continue
            for t8 in range(8):
                half = t8 % 2
                pj = pp if half == 0 else pbc

                def _(t, t8=t8, half=half, pj=pj):
                    W(t, tV, h1_tok)
                    W(t, stW, qkv_tok)
                    W(t, tV, g_["pp_v"][half])
                    W(t, tS, g_["pp_s"][half])
                    for k in range(8):
                        ins = t.matmul(
                            pj[:, 0:SEG],
                            WG[:, D * k + 128 * t8:D * k + 128 * (t8 + 1)],
                            h1s[:, SEG * k:SEG * (k + 1)],
                            start=(k == 0), stop=(k == 7))
                    tT.inc(ins)
                eT(_)
                z_tok = tT.n

                def _(s_, half=half, pj=pj):
                    W(s_, tT, z_tok)
                    W(s_, tV, g_["gate_v"])
                    tS.inc(s_.activation(alph[:, :],
                                         pj[:, 0:SEG],
                                         AF.Sigmoid))
                    tS.inc(s_.activation(inva[:, :],
                                         pj[:, 0:SEG],
                                         AF.Exp, scale=-1.0))
                eS(_)
                ai_tok = tS.n
                g_["pp_s"][half] = ai_tok

                def _(v, t8=t8, par=par):
                    W(v, tS, ai_tok)
                    tV.inc(v.tensor_scalar_add(inva[:, :], inva[:, :], 1.0))
                    for c in range(NCH):
                        tV.inc(v.tensor_tensor_scan(
                            bplus[:, SEG * t8 + CH * c:SEG * t8 + CH * (c + 1)],
                            alph[:, CH * c:CH * (c + 1)],
                            alph[:, CH * c:CH * (c + 1)],
                            1.0, ALU.mult, ALU.bypass))
                    for c in range(NCH):
                        tV.inc(v.tensor_tensor_scan(
                            bminus[:, SEG * t8 + CH * c:SEG * t8 + CH * (c + 1)],
                            inva[:, CH * c:CH * (c + 1)],
                            inva[:, CH * c:CH * (c + 1)],
                            1.0, ALU.mult, ALU.bypass))
                    tV.inc(v.tensor_copy(
                        elcs[:, 8 * NCH * par + NCH * t8:
                             8 * NCH * par + NCH * (t8 + 1)].rearrange(
                            "p (c x) -> p c x", x=1),
                        bplus[:, SEG * t8:SEG * (t8 + 1)].rearrange(
                            "p (c x) -> p c x", x=CH)[:, :, CH - 1:CH]))
                eV(_)
                g_["gate_v"] = tV.n
            gates_tok = g_["gate_v"]

            if a1_stop < 4:
                continue

            def l2_apply(buf, src_tok, bbuf, l2off):
                def _(v):
                    W(v, tS, src_tok)
                    W(v, tT, g_["sqb_t"])
                    for k in range(8):
                        sl = buf[:, sb + SEG * k:sb + SEG * (k + 1)]
                        tV.inc(v.tensor_tensor(sqb[:, SEG * k:SEG * (k + 1)],
                                               sl, sl, ALU.mult))
                eV(_)
                sq2 = tV.n

                def _(t):
                    W(t, tV, sq2)
                    W(t, stConst, const_tok)
                    W(t, tV, g_["pp_v"][1])
                    W(t, tS, g_["pp_s"][1])
                    for k in range(8):
                        ins = t.matmul(pbc[0:16, 0:SEG],
                                       honesB[:, 16 * k:16 * (k + 1)],
                                       sqb[:, SEG * k:SEG * (k + 1)],
                                       start=(k == 0), stop=(k == 7))
                    tT.inc(ins)
                eT(_)
                ss_tok = tT.n
                g_["sqb_t"] = ss_tok

                def _(v):
                    W(v, tT, ss_tok)
                    tV.inc(v.tensor_single_scalar(l2s[:, l2off:l2off + SEG],
                                                  pbc[0:16, 0:SEG],
                                                  1e-24, ALU.max))
                eV(_)
                mx_tok = tV.n
                g_["pp_v"][1] = tV.n

                def _(s_):
                    W(s_, tV, mx_tok)
                    tS.inc(s_.activation(l2s[:, l2off:l2off + SEG],
                                         l2s[:, l2off:l2off + SEG], AF.Ln))
                    tS.inc(s_.activation(l2s[:, l2off:l2off + SEG],
                                         l2s[:, l2off:l2off + SEG],
                                         AF.Exp, scale=-0.5))
                eS(_)
                rsq_tok = tS.n

                last = 0
                for t8 in range(8):
                    pvb = pv if t8 % 2 == 0 else pu
                    gk = "pv_v" if t8 % 2 == 0 else "pu_v"

                    def _(t, t8=t8, pvb=pvb, gk=gk):
                        W(t, tS, rsq_tok)
                        W(t, stConst, const_tok)
                        W(t, tV, g_[gk])
                        tT.inc(t.matmul(pvb[:, 0:SEG],
                                        hexpS[:, 128 * t8:128 * (t8 + 1)],
                                        l2s[:, l2off:l2off + SEG],
                                        start=True, stop=True))
                    eT(_)
                    ex_tok = tT.n

                    def _(v, t8=t8, pvb=pvb, gk=gk):
                        W(v, tT, ex_tok)
                        W(v, tV, gates_tok)
                        sl = buf[:, sb + SEG * t8:sb + SEG * (t8 + 1)]
                        tV.inc(v.tensor_tensor(
                            sl, sl, pvb[:, 0:SEG],
                            ALU.mult))
                        tV.inc(v.tensor_tensor(
                            sl, sl, bbuf[:, SEG * t8:SEG * (t8 + 1)],
                            ALU.mult))
                    eV(_)
                    g_[gk] = tV.n
                    last = tV.n
                return last

            qf_tok = l2_apply(qts, q_tok, bplus, 0)
            kf_tok = l2_apply(kts, k_tok, bminus, SEG)

            if a1_stop < 5:
                continue
            kn_tok = 0
            for t8 in range(8):
                def _(v, t8=t8, par=par):
                    W(v, tV, kf_tok)
                    for c in range(NCH):
                        tV.inc(v.tensor_scalar(
                            khatT[:, CH * c:CH * (c + 1)],
                            kts[:, sb + SEG * t8 + CH * c:
                                sb + SEG * t8 + CH * (c + 1)],
                            elcs[:, 8 * NCH * par + NCH * t8 + c:
                                 8 * NCH * par + NCH * t8 + c + 1],
                            None, ALU.mult))
                eV(_)
                kh_tok = tV.n

                for c in range(NCH):
                    def _(t, c=c):
                        W(t, tV, kh_tok)
                        W(t, stConst, const_tok)
                        W(t, tV, g_["pp_v"][0])
                        W(t, tS, g_["pp_s"][0])
                        tT.inc(t.transpose(pp[0:CH, 256:384],
                                           khatT[:, CH * c:CH * (c + 1)],
                                           identS[:, :]))
                    eT(_)
                    tr_tok = tT.n

                    def _(v, t8=t8, c=c, par=par):
                        W(v, tT, tr_tok)
                        W(v, tT, g_["set_t"][par])
                        p_ = c % 2
                        tt = c // 2
                        tV.inc(v.tensor_copy(
                            kns[64 * p_:64 * p_ + 64,
                                2048 * par + D * tt + 128 * t8:
                                2048 * par + D * tt + 128 * (t8 + 1)],
                            pp[0:CH, 256:384]))
                    eV(_)
                    g_["pp_v"][0] = tV.n
                    kn_tok = tV.n

            if a1_stop < 6:
                continue
            vn_tok = 0
            for tt in range(2):
                for nh in range(2):
                    def _(t, tt=tt, nh=nh):
                        W(t, tV, h1_tok)
                        W(t, stW, qkv_tok)
                        W(t, tV, g_["pv_v"])
                        for k in range(8):
                            ins = t.matmul(
                                pv[:, :],
                                h1s[:, SEG * k + 128 * tt:
                                    SEG * k + 128 * (tt + 1)],
                                WV[:, D * k + 512 * nh:D * k + 512 * (nh + 1)],
                                start=(k == 0), stop=(k == 7))
                        tT.inc(ins)
                    eT(_)
                    vmm_tok = tT.n

                    def _(v, tt=tt, nh=nh, par=par):
                        W(v, tT, vmm_tok)
                        W(v, tT, g_["set_t"][par])
                        tV.inc(v.tensor_copy(
                            vns[:, 2048 * par + D * tt + 512 * nh:
                                2048 * par + D * tt + 512 * (nh + 1)],
                            pv[:, :]))
                    eV(_)
                    g_["pv_v"] = tV.n
                    vn_tok = tV.n

            if a1_stop < 7:
                continue
            oev_tok = 0
            for c in range(NCH):
                cb = 64 * (c % 2)
                tt = c // 2

                def _(v, c=c, par=par):
                    W(v, tT, g_["stil_t"])
                    for t8 in range(8):
                        tV.inc(v.tensor_scalar(
                            Stl[:, CH * t8:CH * (t8 + 1)],
                            Sst[:, CH * t8:CH * (t8 + 1)],
                            elcs[:, 8 * NCH * par + NCH * t8 + c:
                                 8 * NCH * par + NCH * t8 + c + 1],
                            None, ALU.mult))
                eV(_)
                stil_tok = tV.n

                if chunk_ops < 1:
                    continue

                def _(t, c=c, cb=cb):
                    W(t, tV, qf_tok)
                    W(t, tV, kf_tok)
                    W(t, tV, g_["pP_v"])
                    W(t, tT, g_["psb_t"])
                    for h in [0, 2, 4, 6, 8, 10, 12, 14, 1, 3, 5, 7, 9, 11,
                              13, 15]:
                        hb = 64 * (h % 2)
                        col = sb + SEG * (h // 2) + CH * c
                        ins = t.matmul(
                            pP[cb:cb + 64, 64 * h:64 * (h + 1)],
                            kts[hb:hb + 64, col:col + CH],
                            qts[hb:hb + 64, col:col + CH],
                            start=True, stop=True,
                            tile_position=(hb, cb))
                    tT.inc(ins)
                eT(_)
                p_tok = tT.n

                if chunk_ops < 2:
                    g_["pP_v"] = p_tok * 0 + tV.n
                    mask_tok = 0

                def _(v, cb=cb):
                    if chunk_ops < 2:
                        return
                    W(v, tT, p_tok)
                    W(v, tT, prev_O_tok)
                    W(v, stConst, const_tok)
                    tV.inc(v.tensor_tensor(Psb[cb:cb + 64, :],
                                           pP[cb:cb + 64, :],
                                           maskS[cb:cb + 64, :], ALU.mult))
                eV(_)
                if chunk_ops >= 2:
                    mask_tok = tV.n
                    g_["pP_v"] = tV.n

                if chunk_ops < 3:
                    continue

                def _(t, c=c, cb=cb, tt=tt, par=par):
                    W(t, tV, stil_tok)
                    W(t, tV, mask_tok)
                    W(t, tV, vn_tok)
                    W(t, tV, g_["pOa_v"])
                    W(t, tV, g_["pOb_v"])
                    for ph in (0, 1):
                        pOx = pOa if ph == 0 else pOb
                        for i2, h in enumerate(range(ph, NH, 2)):
                            t.matmul(
                                pOx[64 * ph:64 * ph + 64,
                                    64 * (h // 2):64 * (h // 2) + 64],
                                vns[cb:cb + 64,
                                    2048 * par + D * tt + 64 * h:
                                    2048 * par + D * tt + 64 * (h + 1)],
                                Psb[cb:cb + 64, 64 * h:64 * (h + 1)],
                                start=(i2 == 0), stop=False,
                                tile_position=(cb, 64 * ph))
                    for ph in (0, 1):
                        pOx = pOa if ph == 0 else pOb
                        for i2, h in enumerate(range(ph, NH, 2)):
                            ins = t.matmul(
                                pOx[64 * ph:64 * ph + 64,
                                    64 * (h // 2):64 * (h // 2) + 64],
                                Stl[64 * ph:64 * ph + 64,
                                    64 * (h // 2):64 * (h // 2) + 64],
                                qts[64 * ph:64 * ph + 64,
                                    sb + SEG * (h // 2) + CH * c:
                                    sb + SEG * (h // 2) + CH * c + CH],
                                start=False, stop=(i2 == 7),
                                tile_position=(64 * ph, 64 * ph))
                    tT.inc(ins)
                eT(_)
                o_tok = tT.n
                prev_O_tok = o_tok
                g_["psb_t"] = o_tok
                g_["stil_t"] = o_tok

                if chunk_ops < 4:
                    continue

                def _(t, cb=cb, tt=tt, par=par):
                    W(t, tV, kn_tok)
                    W(t, tV, g_["pu_v"])
                    for h in range(NH):
                        hb = 64 * (h % 2)
                        ins = t.matmul(
                            pu[hb:hb + 64, 64 * (h // 2):64 * (h // 2) + 64],
                            kns[cb:cb + 64,
                                2048 * par + D * tt + 64 * h:
                                2048 * par + D * tt + 64 * (h + 1)],
                            vns[cb:cb + 64,
                                2048 * par + D * tt + 64 * h:
                                2048 * par + D * tt + 64 * (h + 1)],
                            start=True, stop=True,
                            tile_position=(cb, hb))
                    tT.inc(ins)
                eT(_)
                u_tok = tT.n

                def _(v, c=c, par=par):
                    W(v, tT, o_tok)
                    W(v, tT, g_["ots_t"][par])
                    ots_v = ots[:, 2048 * par:2048 * par + 2048].rearrange(
                        "p (t x) -> p t x", x=SEG)[:, :, CH * c:CH * (c + 1)]
                    tV.inc(v.tensor_copy(
                        ots_v[0:64],
                        pOa[0:64, :].rearrange("p (t x) -> p t x", x=CH)))
                    tV.inc(v.tensor_copy(
                        ots_v[64:128],
                        pOb[64:128, :].rearrange("p (t x) -> p t x", x=CH)))
                    W(v, tT, u_tok)
                    tV.inc(v.tensor_tensor(Sst, Stl, pu[:, :], ALU.add))
                eV(_)
                g_["pOa_v"] = tV.n
                g_["pOb_v"] = tV.n
                g_["pu_v"] = tV.n
                oev_tok = tV.n

            if a1_stop < 8:
                continue
            g_["h1_t"] = tT.n
            xm_tok = 0
            for mt in range(8):
                half = mt % 2
                pj = pp if half == 0 else pbc

                def _(t, mt=mt, half=half, pj=pj, par=par):
                    W(t, tV, oev_tok)
                    W(t, stW, qkv_tok)
                    W(t, tV, g_["pp_v"][half])
                    W(t, tS, g_["pp_s"][half])
                    for k in range(8):
                        ins = t.matmul(
                            pj[:, 0:SEG],
                            WOx[:, D * k + 128 * mt:D * k + 128 * (mt + 1)],
                            ots[:, 2048 * par + SEG * k:
                                2048 * par + SEG * (k + 1)],
                            start=(k == 0), stop=(k == 7))
                    tT.inc(ins)
                eT(_)
                wo_tok = tT.n
                g_["ots_t"][par] = wo_tok

                def _(v, mt=mt, half=half, pj=pj, par=par):
                    W(v, tT, wo_tok)
                    W(v, stXm, g_["h1_y"])
                    tV.inc(v.tensor_tensor(
                        h1s[:, SEG * mt:SEG * (mt + 1)],
                        x0s[:, 2048 * par + SEG * mt:
                            2048 * par + SEG * (mt + 1)],
                        pj[:, 0:SEG], ALU.add))
                eV(_)
                g_["pp_v"][half] = tV.n
                xm_tok = tV.n
            g_["x0_v"][par] = tV.n
            g_["set_t"][par] = tT.n

            def _(y, so=so):
                W(y, tV, xm_tok)
                stXm.inc(y.dma_start(
                    out=xmid_dram.rearrange("(kt p) t -> p kt t",
                                            p=128)[:, :, so:so + SEG],
                    in_=h1s[:, :].rearrange("p (kt c) -> p kt c", c=SEG)))
            eY(_)
            g_["h1_y"] = stXm.n
            xm_last = stXm.n

        a1_t, a1_v, a1_s = tT.n, tV.n, tS.n
        h2T = ar(O_H2T, 32768)
        w12s = ar(O_W12, 4096)
        x1b = ar(O_X1B, 4096)
        hfr = ar(O_HFR, 11264)
        xmr = ar(O_XMR, 4096)
        xpre = ar(O_XPR, 4096)
        hffw = ar(O_HFW, 2048)
        asr = ar(O_ASR, 1024)
        w3s = ar(O_W3S, 22528)
        lsq = ar(O_LSQ, 4096)

        def _(gq):
            if phase < 4:
                return
            W(gq, tT, a1_t)
            W(gq, tV, a1_v)
            for k in range(NMS):
                rows = 128 if k < NMS - 1 else DFF - 128 * (NMS - 1)
                stW3.inc(gq.dma_start(out=w3s[0:rows, D * k:D * (k + 1)],
                                      in_=w3[128 * k:128 * k + rows, :]))
        eG(_)
        w3ld_tok = stW3.n

        xmr_v = 0
        h2_tok = 0
        for ch in range(8 if phase >= 2 else 0):
            def _(y, ch=ch):
                W(y, tV, a1_v)
                W(y, tT, a1_t)
                W(y, tV, xmr_v)
                W(y, stXm, xm_last)
                stXmr.inc(y.dma_start(
                    out=xmr[:, :].rearrange("p (kt c) -> p kt c", c=512),
                    in_=xmid_dram.rearrange(
                        "(kt p) t -> p kt t",
                        p=128)[:, :, 512 * ch:512 * (ch + 1)]))
            eY(_)
            xl_tok = stXmr.n

            rstd_tok = 0

            def _(v, ch=ch):
                W(v, stXmr, xl_tok)
                W(v, tT, a1_t)
                for k in range(8):
                    sl = xmr[:, 512 * k:512 * (k + 1)]
                    tV.inc(v.tensor_tensor(
                        xpre[:, 512 * k:512 * (k + 1)], sl, sl, ALU.mult))
            eV(_)
            sq_tok = tV.n

            def _(t):
                W(t, tV, sq_tok)
                W(t, tV, g_["pp_v"][0])
                W(t, tS, g_["pp_s"][0])
                for k in range(8):
                    ins = t.matmul(pp[0:1, 0:512], onescolB[:, :],
                                   xpre[:, 512 * k:512 * (k + 1)],
                                   start=(k == 0), stop=(k == 7))
                tT.inc(ins)
            eT(_)
            s1_tok = tT.n
            g_["pp_v"][0] = 0

            def _(s_):
                W(s_, tT, s1_tok)
                W(s_, stConst, const_tok)
                tS.inc(s_.activation(rowst[0:1, 0:512], pp[0:1, 0:512],
                                     AF.Ln, bias=kbias[0:1, 0:1], scale=1.0 / D))
                tS.inc(s_.activation(rowst[0:1, 512:1024],
                                     rowst[0:1, 0:512], AF.Exp, scale=-0.5))
            eS(_)
            rstd_tok = tS.n
            g_["pp_s"][0] = rstd_tok

            for t8 in range(8):
                pvb = pv if t8 % 2 == 0 else pu
                gk = "pv_v" if t8 % 2 == 0 else "pu_v"

                def _(t, t8=t8, pvb=pvb, gk=gk):
                    W(t, tS, rstd_tok)
                    W(t, stConst, const_tok)
                    W(t, tV, g_[gk])
                    tT.inc(t.matmul(pvb[:, :],
                                    gfrow[0:1, 128 * t8:128 * (t8 + 1)],
                                    rowst[0:1, 512:1024],
                                    start=True, stop=True))
                eT(_)
                bc_tok = tT.n

                def _(v, t8=t8, pvb=pvb, ch=ch):
                    W(v, tT, bc_tok)
                    tV.inc(v.tensor_tensor(
                        h2T[:, T * t8 + 512 * ch:T * t8 + 512 * (ch + 1)],
                        xmr[:, 512 * t8:512 * (t8 + 1)],
                        pvb[:, :], ALU.mult))
                eV(_)
                g_[gk] = tV.n
                h2_tok = tV.n
            xmr_v = h2_tok

        w12_free = [0, 0]
        slotA_s = [0, 0]
        slotB_v = [0, 0]
        as_war = [0, 0]
        hff_war = [0, 0]
        hff_wr = [[0] * NMS for _ in range(8)]
        for ms in range(NMS if phase >= 3 else 0):
            mp = ms % 2
            cw = 128 if ms < NMS - 1 else DFF - 128 * (NMS - 1)

            def _(gq, ms=ms, mp=mp, cw=cw):
                W(gq, tT, w12_free[mp])
                W(gq, tT, a1_t)
                W(gq, tV, a1_v)
                stW12[mp].inc(gq.dma_start(
                    out=w12s[:, 2048 * mp:2048 * mp + 8 * cw].rearrange(
                        "p (kt c) -> p kt c", c=cw),
                    in_=w1.rearrange("(kt p) f -> p kt f",
                                     p=128)[:, :, 128 * ms:128 * ms + cw]))
                stW12[mp].inc(gq.dma_start(
                    out=w12s[:, 2048 * mp + 1024:
                             2048 * mp + 1024 + 8 * cw].rearrange(
                        "p (kt c) -> p kt c", c=cw),
                    in_=w2.rearrange("(kt p) f -> p kt f",
                                     p=128)[:, :, 128 * ms:128 * ms + cw]))
            eG(_)
            w_tok = stW12[mp].n

            for tb in range(8):
                q = tb % 2
                pA = pp if q == 0 else pP[:, 0:512]
                pB = pbc if q == 0 else pP[:, 512:1024]
                asl = asr[:, 512 * q:512 * (q + 1)]
                hsl = hffw[:, 512 * q:512 * (q + 1)]

                def _(t, ms=ms, mp=mp, cw=cw, tb=tb, q=q, pA=pA, pB=pB):
                    W(t, stW12[mp], w_tok)
                    W(t, tV, h2_tok)
                    W(t, tS, slotA_s[q])
                    W(t, tV, slotB_v[q])
                    pa = pA[0:cw, :] if cw < 128 else pA[:, :]
                    pb = pB[0:cw, :] if cw < 128 else pB[:, :]
                    for k in range(8):
                        t.matmul(pa,
                                 w12s[:, 2048 * mp + cw * k:
                                      2048 * mp + cw * (k + 1)],
                                 h2T[:, T * k + 512 * tb:T * k + 512 * (tb + 1)],
                                 start=(k == 0), stop=(k == 7))
                    for k in range(8):
                        ins = t.matmul(
                            pb,
                            w12s[:, 2048 * mp + 1024 + cw * k:
                                 2048 * mp + 1024 + cw * (k + 1)],
                            h2T[:, T * k + 512 * tb:T * k + 512 * (tb + 1)],
                            start=(k == 0), stop=(k == 7))
                    tT.inc(ins)
                eT(_)
                ab_tok = tT.n
                w12_free[mp] = tT.n

                def _(s_, cw=cw, q=q, pA=pA, asl=asl):
                    W(s_, tT, ab_tok)
                    W(s_, tV, as_war[q])
                    tS.inc(s_.activation(asl[0:cw, :] if cw < 128 else asl,
                                         pA[0:cw, :] if cw < 128 else pA[:, :],
                                         AF.Silu))
                eS(_)
                si_tok = tS.n
                slotA_s[q] = si_tok

                def _(v, cw=cw, q=q, pB=pB, asl=asl, hsl=hsl):
                    W(v, tS, si_tok)
                    W(v, stHffW[q], hff_war[q])
                    tV.inc(v.tensor_tensor(
                        hsl[0:cw, :] if cw < 128 else hsl,
                        asl[0:cw, :] if cw < 128 else asl,
                        pB[0:cw, :] if cw < 128 else pB[:, :], ALU.mult))
                eV(_)
                hf_tok = tV.n
                slotB_v[q] = hf_tok
                as_war[q] = hf_tok

                def _(y, ms=ms, tb=tb, cw=cw, q=q, hsl=hsl):
                    W(y, tV, hf_tok)
                    stHffW[q].inc(y.dma_start(
                        out=hff_dram[tb, ms, 0:cw, :],
                        in_=hsl[0:cw, :]))
                eY(_)
                hff_war[q] = stHffW[q].n
                hff_wr[tb][ms] = stHffW[q].n

        hfr_t = 0
        x1b_t = 0
        x1b_y = 0
        xps_v = 0
        first_part = True
        sel_v = 0
        part_tok = 0
        for tb in range(8 if phase >= 4 else 0):
            def _(s_, tb=tb):
                W(s_, tT, hfr_t)
                W(s_, stHffW[tb % 2], hff_wr[tb][NMS - 1])
                stHfr.inc(s_.dma_start(
                    out=hfr[:, 0:(NMS - 1) * 512].rearrange(
                        "p (ms c) -> p ms c", c=512),
                    in_=hff_dram[tb, 0:NMS - 1].rearrange("ms p c -> p ms c")))
                rows = DFF - 128 * (NMS - 1)
                stHfr.inc(s_.dma_start(
                    out=hfr[0:rows, (NMS - 1) * 512:NMS * 512],
                    in_=hff_dram[tb, NMS - 1, 0:rows, :]))
            eS(_)
            hl_tok = stHfr.n

            def _(y, tb=tb):
                W(y, tV, xmr_v)
                stXmr.inc(y.dma_start(
                    out=xmr[:, :].rearrange("p (kt c) -> p kt c", c=512),
                    in_=xmid_dram.rearrange(
                        "(kt p) t -> p kt t",
                        p=128)[:, :, 512 * tb:512 * (tb + 1)]))
            eY(_)
            xl_tok = stXmr.n

            xp_tok = 0
            for mt in range(8):
                xs = pP[:, 0:512] if mt % 2 == 0 else pP[:, 512:1024]

                def _(t, mt=mt, xs=xs):
                    W(t, stHfr, hl_tok)
                    W(t, stW3, w3ld_tok)
                    W(t, tV, xps_v)
                    for k in range(NMS):
                        rows = 128 if k < NMS - 1 else DFF - 128 * (NMS - 1)
                        ins = t.matmul(
                            xs[:, :],
                            w3s[0:rows,
                                D * k + 128 * mt:D * k + 128 * (mt + 1)],
                            hfr[0:rows, 512 * k:512 * (k + 1)],
                            start=(k == 0), stop=(k == NMS - 1))
                    tT.inc(ins)
                eT(_)
                x_tok = tT.n

                def _(v, mt=mt, xs=xs):
                    W(v, tT, x_tok)
                    W(v, stXmr, xl_tok)
                    tV.inc(v.tensor_tensor(
                        xpre[:, 512 * mt:512 * (mt + 1)],
                        xmr[:, 512 * mt:512 * (mt + 1)],
                        xs[:, :], ALU.add))
                eV(_)
                xp_tok = tV.n
                if mt % 2 == 1:
                    xps_v = tV.n
            hfr_t = tT.n
            xmr_v = xp_tok
            xps_v = xp_tok

            def _(v):
                W(v, tV, xp_tok)
                for k in range(8):
                    sl = xpre[:, 512 * k:512 * (k + 1)]
                    tV.inc(v.tensor_tensor(
                        lsq[:, 512 * k:512 * (k + 1)], sl, sl, ALU.mult))
            eV(_)
            lsq_tok = tV.n

            def _(t):
                W(t, tV, lsq_tok)
                W(t, tV, g_["pp_v"][0])
                W(t, tS, g_["pp_s"][0])
                for k in range(8):
                    ins = t.matmul(pp[0:1, 0:512], onescolB[:, :],
                                   xpre[:, 512 * k:512 * (k + 1)],
                                   start=(k == 0), stop=(k == 7))
                tT.inc(ins)
                W(t, tV, g_["pbc_v"][0])
                W(t, tV, g_["pbc_v"][1])
                for k in range(8):
                    ins = t.matmul(pbc[0:1, 0:512], onescolB[:, :],
                                   lsq[:, 512 * k:512 * (k + 1)],
                                   start=(k == 0), stop=(k == 7))
                tT.inc(ins)
            eT(_)
            st_tok = tT.n

            def _(v):
                W(v, tT, st_tok)
                tV.inc(v.tensor_single_scalar(rowst[0:1, 0:512],
                                              pp[0:1, 0:512],
                                              1.0 / D, ALU.mult))
                tV.inc(v.tensor_single_scalar(rowst[0:1, 512:1024],
                                              pbc[0:1, 0:512],
                                              1.0 / D, ALU.mult))
                tV.inc(v.tensor_tensor(rowst[0:1, 1536:2048],
                                       rowst[0:1, 0:512],
                                       rowst[0:1, 0:512], ALU.mult))
                tV.inc(v.tensor_tensor(rowst[0:1, 512:1024],
                                       rowst[0:1, 512:1024],
                                       rowst[0:1, 1536:2048], ALU.subtract))
            eV(_)
            var_tok = tV.n
            g_["pp_v"][0] = var_tok
            g_["pp_s"][0] = 0
            g_["pbc_v"][0] = var_tok
            g_["pbc_v"][1] = var_tok

            def _(s_):
                W(s_, tV, var_tok)
                W(s_, stConst, const_tok)
                tS.inc(s_.activation(rowst[0:1, 512:1024],
                                     rowst[0:1, 512:1024], AF.Ln,
                                     bias=kbias[0:1, 1:2]))
                tS.inc(s_.activation(rowst[0:1, 512:1024],
                                     rowst[0:1, 512:1024], AF.Exp, scale=-0.5))
            eS(_)
            ln_tok = tS.n

            def _(v):
                W(v, tS, ln_tok)
                tV.inc(v.tensor_tensor(rowst[0:1, 1024:1536],
                                       rowst[0:1, 0:512],
                                       rowst[0:1, 512:1024], ALU.mult))
            eV(_)
            bm_tok = tV.n

            def _(t):
                W(t, tV, bm_tok)
                W(t, stConst, const_tok)
                W(t, tV, g_["pv_v"])
                W(t, tV, g_["pu_v"])
                tT.inc(t.matmul(pv[:, :], onesrowS[:, :],
                                rowst[0:1, 512:1024], start=True, stop=True))
                tT.inc(t.matmul(pu[:, :], onesrowS[:, :],
                                rowst[0:1, 1024:1536], start=True, stop=True))
            eT(_)
            bc2_tok = tT.n

            for mt in range(8):
                def _(v, mt=mt):
                    W(v, tT, bc2_tok)
                    W(v, tT, x1b_t)
                    W(v, stX1, x1b_y)
                    dst = x1b[:, 512 * mt:512 * (mt + 1)]
                    tV.inc(v.tensor_tensor(dst,
                                           xpre[:, 512 * mt:512 * (mt + 1)],
                                           pv[:, :], ALU.mult))
                    tV.inc(v.tensor_tensor(dst, dst, pu[:, :], ALU.subtract))
                eV(_)
            g_["pv_v"] = tV.n
            g_["pu_v"] = tV.n
            x1_tok = tV.n

            def _(y, tb=tb):
                W(y, tV, x1_tok)
                stX1.inc(y.dma_start(
                    out=x1T_out.rearrange("(kt p) t -> p kt t",
                                          p=128)[:, :, 512 * tb:512 * (tb + 1)],
                    in_=x1b[:, :].rearrange("p (kt c) -> p kt c", c=512)))
            eY(_)
            x1b_y = stX1.n

            for j in range(4):
                def _(t, j=j):
                    W(t, tV, x1_tok)
                    W(t, stW, qkv_tok)
                    W(t, tV, g_["pOa_v"])
                    for k in range(8):
                        ins = t.matmul(
                            pOa[:, 32 * j:32 * j + 8],
                            x1b[:, 512 * k + 128 * j:512 * k + 128 * (j + 1)],
                            routs[:, 8 * k:8 * (k + 1)],
                            start=(k == 0), stop=(k == 7))
                    tT.inc(ins)
                eT(_)
                r_tok = tT.n

                def _(v, j=j, tb=tb):
                    W(v, tT, r_tok)
                    rps = pOa[:, 32 * j:32 * j + 8]
                    tt = 4 * tb + j
                    for op in (
                        lambda: v.reduce_max(rtm[:, 0:1], rps, axis=AX.X),
                        lambda: v.tensor_scalar(rcols[:, :], rps,
                                                rtm[:, 0:1], None, ALU.is_ge),
                        lambda: v.scalar_tensor_tensor(
                            selw[:, 0:8], rcols[:, :], -1e30, rps,
                            ALU.mult, ALU.add),
                        lambda: v.reduce_max(rtm[:, 1:2], selw[:, 0:8],
                                             axis=AX.X),
                        lambda: v.tensor_scalar(selw[:, 8:16], selw[:, 0:8],
                                                rtm[:, 1:2], None, ALU.is_ge),
                        lambda: v.tensor_tensor(
                            selacc[:, 8 * tt:8 * (tt + 1)], rcols[:, :],
                            selw[:, 8:16], ALU.add),
                        lambda: v.tensor_tensor(rtm[:, 2:3], rtm[:, 1:2],
                                                rtm[:, 0:1], ALU.subtract),
                    ):
                        tV.inc(op())
                        v.wait_ge(s_v, tV.n)
                eV(_)
                t2a_tok = tV.n

                def _(s_):
                    W(s_, tV, t2a_tok)
                    tS.inc(s_.activation(rtm[:, 3:4], rtm[:, 2:3], AF.Exp))
                eS(_)
                ed_tok = tS.n

                def _(v):
                    W(v, tS, ed_tok)
                    for op in (
                        lambda: v.tensor_scalar_add(rtm[:, 4:5],
                                                    rtm[:, 3:4], 1.0),
                        lambda: v.reciprocal(rtm[:, 5:6], rtm[:, 4:5]),
                        lambda: v.tensor_single_scalar(rtm[:, 6:7],
                                                       rtm[:, 0:1],
                                                       -1.0, ALU.mult),
                    ):
                        tV.inc(op())
                        v.wait_ge(s_v, tV.n)
                eV(_)
                pr_tok = tV.n

                def _(s_, j=j):
                    W(s_, tV, pr_tok)
                    tS.inc(s_.activation(selw[:, 0:8],
                                         pOa[:, 32 * j:32 * j + 8],
                                         AF.Exp, bias=rtm[:, 6:7]))
                eS(_)
                el_tok = tS.n

                def _(v, tb=tb, j=j):
                    W(v, tS, el_tok)
                    tt = 4 * tb + j
                    for op in (
                        lambda: v.tensor_scalar(selw[:, 0:8], selw[:, 0:8],
                                                rtm[:, 5:6], None, ALU.mult),
                        lambda: v.tensor_tensor(
                            selw[:, 0:8], selw[:, 0:8],
                            selacc[:, 8 * tt:8 * (tt + 1)], ALU.mult),
                        lambda: v.tensor_copy(selw[:, 8:16],
                                              selacc[:, 8 * tt:8 * (tt + 1)]),
                    ):
                        tV.inc(op())
                        v.wait_ge(s_v, tV.n)
                eV(_)
                w_tok2 = tV.n
                g_["pOa_v"] = tV.n
                sel_v = tV.n

                def _(t, tb=tb, j=j):
                    W(t, tV, w_tok2)
                    W(t, stConst, const_tok)
                    W(t, tV, g_["pOb_v"])
                    tT.inc(t.matmul(pOb[0:16, 0:1], selw[:, :],
                                    onescolS[:, :],
                                    start=first_part,
                                    stop=(tb == 7 and j == 3),
                                    skip_group_check=True))
                eT(_)
                first_part = False
                part_tok = tT.n
            x1b_t = tT.n

        if phase >= 4:
            def _(v):
                W(v, tT, part_tok)
                tV.inc(v.tensor_copy(ptsb[:, :], pOb[0:16, 0:1]))
            eV(_)
            pts_tok = tV.n

            def _(y):
                W(y, tV, pts_tok)
                W(y, tV, sel_v)
                stOut.inc(y.dma_start(out=sel_out[:, :], in_=selacc[:, :]))
                stOut.inc(y.dma_start(
                    out=part_out[0:1, 0:8].rearrange("a c -> c a"),
                    in_=ptsb[0:8, 0:1]))
                stOut.inc(y.dma_start(
                    out=part_out[0:1, 32:40].rearrange("a c -> c a"),
                    in_=ptsb[8:16, 0:1]))
            eY(_)
            out_tok = stOut.n

        def _(y):
            if phase >= 4:
                y.wait_ge(stOut.sem, stOut.n)
                y.wait_ge(stX1.sem, stX1.n)
            y.wait_ge(stXm.sem, stXm.n)
            y.wait_ge(s_v, tV.n)
            y.wait_ge(s_s, tS.n)
            y.wait_ge(s_t, tT.n)
        eY(_)

    return nc


def attn_inputs(x_b, inp, consts):
    m = {"c_" + k: v for k, v in consts.items()}
    m["x0T"] = np.ascontiguousarray(x_b.T)
    m["Wq"] = inp["L0_attn_Wq"]
    m["Wk"] = inp["L0_attn_Wk"]
    m["Wv"] = inp["L0_attn_Wv"]
    m["Wg"] = inp["L0_attn_Wg"]
    m["Wo"] = inp["L0_attn_Wo"]
    m["w1"] = inp["L0_attn_w1"]
    m["w2"] = inp["L0_attn_w2"]
    m["w3"] = inp["L0_attn_w3"]
    m["g_attn"] = inp["L0_attn_g_attn"].reshape(1, D)
    m["g_ffn"] = inp["L0_attn_g_ffn"].reshape(1, D)
    m["routerT"] = np.ascontiguousarray(inp["L0_ffn_router_w"].T)
    return m




M_X1T = 0
M_DW0 = 32768
M_DW1 = 49152
M_DW2 = 65536
M_GW = 81920
M_HIN = 86016
M_HOUT = 94208
M_GS = 95232
M_EV = 96256
M_AR2 = 97280
M_VEOR = 0
M_XNR = 16384


def build_moe(races=True):
    nc = bass.Bass("TRN2", target_bir_lowering=False,
                   detect_race_conditions=races)
    x1T = nc.dram_tensor("x1T", [D, T], BF16, kind="ExternalInput").ap()
    x1nat = nc.dram_tensor("x1nat", [T, D], BF16, kind="ExternalInput").ap()
    sel_in = nc.dram_tensor("sel", [128, 32 * 8], F32,
                            kind="ExternalInput").ap()
    pall = nc.dram_tensor("partials_all", [8, 40], F32,
                          kind="ExternalInput").ap()
    gwT = [nc.dram_tensor(f"gwT{e}", [D, DI], F32, kind="ExternalInput").ap()
           for e in range(NE)]
    uwT = [nc.dram_tensor(f"uwT{e}", [D, DI], F32, kind="ExternalInput").ap()
           for e in range(NE)]
    dwT = [nc.dram_tensor(f"dwT{e}", [DI, D], F32, kind="ExternalInput").ap()
           for e in range(NE)]
    ones_row = nc.dram_tensor("c_ones_row", [1, 128], F32,
                              kind="ExternalInput").ap()
    kb = nc.dram_tensor("c_biasvals", [128, 4], F32,
                        kind="ExternalInput").ap()
    out = nc.dram_tensor("out", [T, D], F32, kind="ExternalOutput").ap()
    hdram = [nc.dram_tensor(f"h_scr{i}", [32, 128, 32, 128], BF16).ap()
             for i in range(2)]
    veod = [nc.dram_tensor(f"veo_scr{e}", [T, D], BF16).ap()
            for e in range(NE)]
    mwd = nc.dram_tensor("mw_scr", [1, 8], F32).ap()

    es = contextlib.ExitStack()
    with es:
        AR = es.enter_context(nc.sbuf_tensor("ARM", [128, M_AR2], BF16))
        accf = es.enter_context(nc.sbuf_tensor("accf", [128, 2 * D], F32))
        sqf = es.enter_context(nc.sbuf_tensor("sqf", [128, D], F32))
        outb = es.enter_context(nc.sbuf_tensor("outb", [128, D], F32))
        sels = es.enter_context(nc.sbuf_tensor("sels", [128, 256], F32))
        pab = es.enter_context(nc.sbuf_tensor("pab", [40, 8], F32))
        pas = es.enter_context(nc.sbuf_tensor("pas", [40, 1], F32))
        mws = es.enter_context(nc.sbuf_tensor("mws", [8, 1], F32))
        mwrow = es.enter_context(nc.sbuf_tensor("mwrow", [1, 8], F32))
        mwbc = es.enter_context(nc.sbuf_tensor("mwbc", [128, 8], F32))
        rowst = es.enter_context(nc.sbuf_tensor("rowst", [128, 8], F32))
        onesrowS = es.enter_context(nc.sbuf_tensor("onesrowS", [1, 128], F32))
        kbias = es.enter_context(nc.sbuf_tensor("kbias", [128, 4], F32))
        pp = es.enter_context(nc.psum_tensor("pp", [128, 512], F32))
        pbc = es.enter_context(nc.psum_tensor("pbc", [128, 512], F32))
        pv = es.enter_context(nc.psum_tensor("pv", [128, 512], F32))
        pu = es.enter_context(nc.psum_tensor("pu", [128, 512], F32))
        pP = es.enter_context(nc.psum_tensor("pP", [128, 1024], F32))
        pOa = es.enter_context(nc.psum_tensor("pOa", [128, 512], F32))
        pOb = es.enter_context(nc.psum_tensor("pOb", [128, 512], F32))
        s_t = es.enter_context(nc.semaphore("s_t"))
        s_v = es.enter_context(nc.semaphore("s_v"))
        s_s = es.enter_context(nc.semaphore("s_s"))

        def mkq(name):
            return Toks(es.enter_context(nc.semaphore(name)), 16)

        stIn = mkq("stIn")
        stMw = mkq("stMw")
        stGW = [mkq("stGWa"), mkq("stGWb")]
        stDW = [mkq("stDW0"), mkq("stDW1"), mkq("stDW2")]
        stHw = [mkq("stHwa"), mkq("stHwb")]
        stHr = [mkq("stHra"), mkq("stHrb")]
        stVW = [mkq("stVWa"), mkq("stVWb")]
        stVR = [mkq("stVRa"), mkq("stVRb")]
        stXn = [mkq("stXna"), mkq("stXnb")]
        stOut = [mkq("stOuta"), mkq("stOutb")]
        block = es.enter_context(nc.Block())

        tT, tV, tS = Toks(s_t, 1), Toks(s_v, 1), Toks(s_s, 1)
        W = Waiter()
        eT, eV, eS, eG, eY = (block.tensor, block.vector, block.scalar,
                              block.gpsimd, block.sync)

        def ar(off, n):
            return AR[:, off:off + n]

        x1s = ar(M_X1T, 32768)
        dws = [ar(M_DW0, 16384), ar(M_DW1, 16384), ar(M_DW2, 16384)]
        gws = ar(M_GW, 4096)
        hins = ar(M_HIN, 8192)
        houts = ar(M_HOUT, 1024)
        gss = ar(M_GS, 1024)
        evs = ar(M_EV, 1024)
        veoR = ar(M_VEOR, 16384)
        xnR = ar(M_XNR, 2048)

        def _(y):
            stIn.inc(y.dma_start(
                out=x1s[:, :].rearrange("p (kt t) -> p kt t", t=T),
                in_=x1T.rearrange("(kt p) t -> p kt t", p=128)))
            stIn.inc(y.dma_start(out=sels[:, :], in_=sel_in))
            stIn.inc(y.dma_start(out=onesrowS[:, :], in_=ones_row))
            stIn.inc(y.dma_start(out=kbias[:, :], in_=kb))
            stMw.inc(y.dma_start(out=pab[:, :],
                                 in_=pall.rearrange("r c -> c r"),
                                 allow_slow_non_contiguous=True))
        eY(_)
        in_tok = stIn.n
        pa_tok = stMw.n

        def _(v):
            W(v, stMw, pa_tok)
            for op in (
                lambda: v.reduce_sum(pas[:, :], pab[:, :], axis=AX.X),
                lambda: v.tensor_single_scalar(mws[:, :], pas[32:40, 0:1],
                                               1.0, ALU.max),
                lambda: v.reciprocal(mws[:, :], mws[:, :]),
                lambda: v.tensor_tensor(mws[:, :], pas[0:8, 0:1], mws[:, :],
                                        ALU.mult),
            ):
                tV.inc(op())
                v.wait_ge(s_v, tV.n)
        eV(_)
        mw_v = tV.n

        def _(y):
            W(y, tV, mw_v)
            stMw.inc(y.dma_start(out=mwd.rearrange("a c -> c a"),
                                 in_=mws[:, :]))
        eY(_)
        mwb1 = stMw.n

        def _(y):
            W(y, stMw, mwb1)
            stMw.inc(y.dma_start(out=mwrow[:, :], in_=mwd))
        eY(_)
        mwb2 = stMw.n

        def _(t):
            W(t, stMw, mwb2)
            W(t, stIn, in_tok)
            tT.inc(t.matmul(pOa[:, 0:8], onesrowS[:, :], mwrow[:, :],
                            start=True, stop=True))
        eT(_)
        mwm_tok = tT.n

        def _(v):
            W(v, tT, mwm_tok)
            tV.inc(v.tensor_copy(mwbc[:, :], pOa[:, 0:8]))
        eV(_)
        mwbc_tok = tV.n

        gg = {"pG_s": [0, 0], "pU_v": [0, 0], "gs_v": [0, 0],
              "hout_y": [0, 0], "gw_t": [0, 0], "dw_t": [0, 0, 0],
              "hin_t": [0, 0], "pD_v": [0, 0], "ev_y": [0, 0],
              "veoR_v": [0, 0], "xn_v": [0, 0], "acc_y": [0, 0],
              "out_y": [0, 0]}
        dw_ld = [0] * 16
        veo7 = [[0, 0] for _ in range(32)]
        gu_end_t = [0] * NE
        hw_tok = [[0] * 2 for _ in range(8)]

        def load_dw(eng_is_g, hh):
            e, h = hh // 2, hh % 2
            buf = dws[hh % 3]

            def _(gq, e=e, h=h, buf=buf, hh=hh):
                W(gq, tT, gg["dw_t"][hh % 3])
                for q in range(4):
                    stDW[hh % 3].inc(gq.dma_start(
                        out=buf[:, 4096 * q:4096 * (q + 1)].rearrange(
                            "p (it c) -> p it c", c=512),
                        in_=dwT[e].rearrange(
                            "(it p) c -> p it c",
                            p=128)[:, 8 * q:8 * (q + 1), 512 * h:512 * (h + 1)]))
            eG(_)
            dw_ld[hh] = stDW[hh % 3].n

        load_dw(True, 0)
        load_dw(True, 1)
        load_dw(True, 2)

        for e in range(NE):
            for i in range(32):
                sl = i % 2

                def _(gq, e=e, i=i, sl=sl):
                    W(gq, tT, gg["gw_t"][sl])
                    stGW[sl].inc(gq.dma_start(
                        out=gws[:, 2048 * sl:2048 * sl + 1024].rearrange(
                            "p (kt c) -> p kt c", c=128),
                        in_=gwT[e].rearrange(
                            "(kt p) f -> p kt f",
                            p=128)[:, :, 128 * i:128 * (i + 1)]))
                    stGW[sl].inc(gq.dma_start(
                        out=gws[:, 2048 * sl + 1024:
                                2048 * sl + 2048].rearrange(
                            "p (kt c) -> p kt c", c=128),
                        in_=uwT[e].rearrange(
                            "(kt p) f -> p kt f",
                            p=128)[:, :, 128 * i:128 * (i + 1)]))
                eG(_)
                gw_tok = stGW[sl].n

                for tb in range(8):
                    q = tb % 2
                    pG = pp if q == 0 else pbc
                    pU = pv if q == 0 else pu

                    def _(t, e=e, i=i, sl=sl, tb=tb, q=q, pG=pG, pU=pU):
                        W(t, stGW[sl], gw_tok)
                        W(t, stIn, in_tok)
                        W(t, tS, gg["pG_s"][q])
                        W(t, tV, gg["pU_v"][q])
                        for k in range(8):
                            t.matmul(pG[:, :],
                                     gws[:, 2048 * sl + 128 * k:
                                         2048 * sl + 128 * (k + 1)],
                                     x1s[:, T * k + 512 * tb:
                                         T * k + 512 * (tb + 1)],
                                     start=(k == 0), stop=(k == 7))
                        for k in range(8):
                            ins = t.matmul(
                                pU[:, :],
                                gws[:, 2048 * sl + 1024 + 128 * k:
                                    2048 * sl + 1024 + 128 * (k + 1)],
                                x1s[:, T * k + 512 * tb:
                                    T * k + 512 * (tb + 1)],
                                start=(k == 0), stop=(k == 7))
                        tT.inc(ins)
                    eT(_)
                    mm_tok = tT.n

                    def _(s_, q=q, pG=pG):
                        W(s_, tT, mm_tok)
                        W(s_, tV, gg["gs_v"][q])
                        tS.inc(s_.activation(
                            gss[:, 512 * q:512 * (q + 1)], pG[:, :], AF.Silu))
                    eS(_)
                    si_tok = tS.n
                    gg["pG_s"][q] = si_tok

                    def _(v, q=q, pU=pU):
                        W(v, tS, si_tok)
                        W(v, stHw[q], gg["hout_y"][q])
                        tV.inc(v.tensor_tensor(
                            houts[:, 512 * q:512 * (q + 1)],
                            gss[:, 512 * q:512 * (q + 1)], pU[:, :], ALU.mult))
                    eV(_)
                    h_tok = tV.n
                    gg["pU_v"][q] = h_tok
                    gg["gs_v"][q] = h_tok

                    def _(y, e=e, i=i, tb=tb, q=q):
                        W(y, tV, h_tok)
                        stHw[q].inc(y.dma_start(
                            out=hdram[e % 2][4 * tb:4 * (tb + 1), :, i,
                                             :].rearrange("j p c -> p j c"),
                            in_=houts[:, 512 * q:512 * (q + 1)].rearrange(
                                "p (j c) -> p j c", c=128)))
                    eY(_)
                    gg["hout_y"][q] = stHw[q].n
                    hw_tok[tb][0 if e % 2 == 0 else 1] = stHw[q].n
                gg["gw_t"][sl] = tT.n
            gu_end_t[e] = tT.n
            h_done = [stHw[0].n, stHw[1].n]

            for tt in range(32):
                hsl = tt % 2

                def _(s_, e=e, tt=tt, hsl=hsl):
                    W(s_, tT, gg["hin_t"][hsl])
                    W(s_, stHw[0], h_done[0])
                    W(s_, stHw[1], h_done[1])
                    stHr[hsl].inc(s_.dma_start(
                        out=hins[:, 4096 * hsl:4096 * (hsl + 1)],
                        in_=hdram[e % 2][tt].rearrange("p i c -> p (i c)")))
                eS(_)
                hl_tok = stHr[hsl].n

                for h2 in range(2):
                    hh = 2 * e + h2
                    dbuf = dws[hh % 3]
                    pD = pP[:, 0:512] if (2 * tt + h2) % 2 == 0                         else pP[:, 512:1024]
                    dq = (2 * tt + h2) % 2

                    def _(t, e=e, tt=tt, hsl=hsl, h2=h2, hh=hh, dbuf=dbuf,
                          pD=pD, dq=dq):
                        W(t, stHr[hsl], hl_tok)
                        W(t, stDW[hh % 3], dw_ld[hh])
                        W(t, tV, gg["pD_v"][dq])
                        for i in range(32):
                            ins = t.matmul(
                                pD,
                                hins[:, 4096 * hsl + 128 * i:
                                     4096 * hsl + 128 * (i + 1)],
                                dbuf[:, 512 * i:512 * (i + 1)],
                                start=(i == 0), stop=(i == 31))
                        tT.inc(ins)
                    eT(_)
                    dmm_tok = tT.n

                    def _(v, e=e, tt=tt, h2=h2, pD=pD, dq=dq):
                        W(v, tT, dmm_tok)
                        W(v, stVW[dq], gg["ev_y"][dq])
                        tV.inc(v.tensor_scalar(
                            evs[:, 512 * dq:512 * (dq + 1)], pD,
                            sels[:, 8 * tt + e:8 * tt + e + 1],
                            None, ALU.mult))
                    eV(_)
                    ev_tok = tV.n
                    gg["pD_v"][dq] = ev_tok

                    def _(y, e=e, tt=tt, h2=h2, dq=dq):
                        W(y, tV, ev_tok)
                        stVW[dq].inc(y.dma_start(
                            out=veod[e][128 * tt:128 * (tt + 1),
                                        512 * h2:512 * (h2 + 1)],
                            in_=evs[:, 512 * dq:512 * (dq + 1)]))
                    eY(_)
                    gg["ev_y"][dq] = stVW[dq].n
                    if e == NE - 1:
                        veo7[tt][h2] = (dq, stVW[dq].n)
                gg["hin_t"][hsl] = tT.n
            for k3 in range(3):
                gg["dw_t"][k3] = tT.n if (2 * e) % 3 == k3 or                     (2 * e + 1) % 3 == k3 else gg["dw_t"][k3]
            if 2 * e + 3 < 2 * NE:
                load_dw(True, 2 * e + 3)
            if 2 * e + 4 < 2 * NE:
                load_dw(True, 2 * e + 4)

        x1_dead_t = gu_end_t[NE - 1]

        for tt in range(32):
            fs = tt % 2

            def _(y, tt=tt, fs=fs):
                W(y, tT, x1_dead_t)
                W(y, tV, gg["veoR_v"][fs])
                for e in range(NE):
                    if e == NE - 1:
                        for h2 in range(2):
                            dq, tok = veo7[tt][h2]
                            W(y, stVW[dq], tok)
                    stVR[fs].inc(y.dma_start(
                        out=veoR[:, 8192 * fs + 1024 * e:
                                 8192 * fs + 1024 * (e + 1)],
                        in_=veod[e][128 * tt:128 * (tt + 1), :]))
            eY(_)
            vr_tok = stVR[fs].n

            def _(y, tt=tt, fs=fs):
                W(y, tT, x1_dead_t)
                W(y, tV, gg["xn_v"][fs])
                stXn[fs].inc(y.dma_start(
                    out=xnR[:, 1024 * fs:1024 * (fs + 1)],
                    in_=x1nat[128 * tt:128 * (tt + 1), :]))
            eY(_)
            xn_tok = stXn[fs].n

            def _(v, tt=tt, fs=fs):
                W(v, stVR[fs], vr_tok)
                W(v, stXn[fs], xn_tok)
                W(v, tV, mwbc_tok)
                W(v, stOut[fs], gg["out_y"][fs])
                acc = accf[:, D * fs:D * (fs + 1)]
                tV.inc(v.scalar_tensor_tensor(
                    acc, veoR[:, 8192 * fs:8192 * fs + 1024],
                    mwbc[:, 0:1], xnR[:, 1024 * fs:1024 * (fs + 1)],
                    ALU.mult, ALU.add))
                v.wait_ge(s_v, tV.n)
                for e in range(1, NE):
                    tV.inc(v.scalar_tensor_tensor(
                        acc, veoR[:, 8192 * fs + 1024 * e:
                                  8192 * fs + 1024 * (e + 1)],
                        mwbc[:, e:e + 1], acc, ALU.mult, ALU.add))
                    v.wait_ge(s_v, tV.n)
                tV.inc(v.reduce_sum(rowst[:, 0:1], acc, axis=AX.X))
                v.wait_ge(s_v, tV.n)
            eV(_)
            s1_tok = tV.n

            def _(s_, fs=fs):
                W(s_, tV, s1_tok)
                tS.inc(s_.activation(sqf[:, :], accf[:, D * fs:D * (fs + 1)],
                                     AF.Square))
            eS(_)
            sq_tok = tS.n

            def _(v, fs=fs):
                W(v, tS, sq_tok)
                for op in (
                    lambda: v.reduce_sum(rowst[:, 1:2], sqf[:, :], axis=AX.X),
                    lambda: v.tensor_single_scalar(rowst[:, 2:3],
                                                   rowst[:, 0:1],
                                                   1.0 / D, ALU.mult),
                    lambda: v.tensor_single_scalar(rowst[:, 3:4],
                                                   rowst[:, 1:2],
                                                   1.0 / D, ALU.mult),
                    lambda: v.tensor_tensor(rowst[:, 4:5], rowst[:, 2:3],
                                            rowst[:, 2:3], ALU.mult),
                    lambda: v.tensor_tensor(rowst[:, 5:6], rowst[:, 3:4],
                                            rowst[:, 4:5], ALU.subtract),
                ):
                    tV.inc(op())
                    v.wait_ge(s_v, tV.n)
            eV(_)
            var_tok = tV.n

            def _(s_):
                W(s_, tV, var_tok)
                W(s_, stIn, in_tok)
                tS.inc(s_.activation(rowst[:, 6:7], rowst[:, 5:6], AF.Ln,
                                     bias=kbias[:, 1:2]))
                s_.wait_ge(s_s, tS.n)
                tS.inc(s_.activation(rowst[:, 6:7], rowst[:, 6:7], AF.Exp,
                                     scale=-0.5))
            eS(_)
            rstd_tok = tS.n

            def _(v, fs=fs):
                W(v, tS, rstd_tok)
                W(v, stOut[0], gg["out_y"][0])
                W(v, stOut[1], gg["out_y"][1])
                tV.inc(v.tensor_scalar(outb[:, :],
                                       accf[:, D * fs:D * (fs + 1)],
                                       rowst[:, 2:3], rowst[:, 6:7],
                                       ALU.subtract, ALU.mult))
                v.wait_ge(s_v, tV.n)
            eV(_)
            o_tok = tV.n
            gg["veoR_v"][fs] = o_tok
            gg["xn_v"][fs] = o_tok

            def _(y, tt=tt, fs=fs):
                W(y, tV, o_tok)
                stOut[fs].inc(y.dma_start(
                    out=out[128 * tt:128 * (tt + 1), :],
                    in_=outb[:, :]))
            eY(_)
            gg["out_y"][fs] = stOut[fs].n

        def _(y):
            y.wait_ge(stOut[0].sem, stOut[0].n)
            y.wait_ge(stOut[1].sem, stOut[1].n)
        eY(_)

    return nc


def moe_inputs(x1T_np, sel_np, pall_np, inp, consts):
    m = {
        "x1T": x1T_np,
        "x1nat": np.ascontiguousarray(x1T_np.T),
        "sel": sel_np,
        "partials_all": pall_np,
        "c_ones_row": consts["ones_row"],
        "c_biasvals": np.tile(consts["biasvals"], (128, 1)),
    }
    for e in range(NE):
        m[f"gwT{e}"] = np.ascontiguousarray(inp["L0_ffn_gate_w"][e].T)
        m[f"uwT{e}"] = np.ascontiguousarray(inp["L0_ffn_up_w"][e].T)
        m[f"dwT{e}"] = np.ascontiguousarray(inp["L0_ffn_down_w"][e].T)
    return m


_CACHE = {}


def kernel(**inputs):
    consts = host_consts()
    if "attn" not in _CACHE:
        _CACHE["attn"] = build_attn()
        _CACHE["moe"] = build_moe()
    nc_a, nc_m = _CACHE["attn"], _CACHE["moe"]
    x = inputs["L0_attn_x"].astype(np.float32)
    maps_a = [attn_inputs(x[b], inputs, consts) for b in range(8)]
    res_a = run_bass_kernel_spmd(nc_a, maps_a, core_ids=list(range(8)))
    pall = np.concatenate([res_a.results[b]["partials"]
                           for b in range(8)], axis=0)
    maps_m = [moe_inputs(res_a.results[b]["x1T"], res_a.results[b]["sel"],
                         pall, inputs, consts) for b in range(8)]
    res_m = run_bass_kernel_spmd(nc_m, maps_m, core_ids=list(range(8)))
    return np.stack([res_m.results[b]["out"] for b in range(8)], axis=0)


if __name__ == "__main__":
    import time
    t0 = time.time()
    nc = build_attn()
    n = sum(1 for _ in nc.all_instructions())
    print(f"built in {time.time()-t0:.1f}s, {n} instructions")
